# version 03 (final)
# baseline (speedup 1.0000x reference)
import numpy as np
import ml_dtypes
from contextlib import ExitStack

import concourse.bass as bass
import concourse.mybir as mybir
from concourse.bass_utils import run_bass_kernel_spmd
from concourse.bass_core_rust import IndirectOffsetOnAxis

F32 = mybir.dt.float32
BF16 = mybir.dt.bfloat16
I32 = mybir.dt.int32
U8 = mybir.dt.uint8
AF = mybir.ActivationFunctionType
ALU = mybir.AluOpType
AX = mybir.AxisListType
BF16NP = ml_dtypes.bfloat16

T = 4096
HALO = 3
DM = 2048
NJ = 16
NTT = 8
D_INNER = 2048
N_HEADS = 32
DS = 64
DP = 64
NCH = 32
CONV_DIM = 6144
OXBC = 48
OZ = 16
IN_OUT = 8224
NE = 16
CAPE = 1280
NSL = CAPE // 128
IE = 1408
UPO = 2816
NUC = UPO // 128
NIC = IE // 128
SIU = 5632
NSUC = SIU // 128
SID = 2816
NSDC = SID // 128
XNW = 4224
NEGBIG = -1e30
ENGINES = ("sync", "gpsimd", "vector", "scalar", "tensor")


class Sem:
    def __init__(self, h):
        self.h = h
        self.n = 0


class It:
    __slots__ = ("eng", "fn", "waits", "sig", "amt", "val", "tag", "drain")

    def __init__(self, eng, fn, waits, sig, amt, tag, drain=False):
        self.eng, self.fn, self.waits, self.sig, self.amt, self.tag = (
            eng, fn, list(waits), sig, amt, tag)
        self.val = None
        self.drain = drain


class Ev:
    __slots__ = ("sem", "item")

    def __init__(self, sem, item):
        self.sem, self.item = sem, item

    @property
    def val(self):
        assert self.item.val is not None
        return self.item.val


class Ph:

    def __init__(self, c):
        self.c = c
        self.items = []
        self.es = ExitStack()
        if hasattr(c, "qbase"):
            c.qcursor = dict(c.qbase)

    def sb(self, name, shape, dt=F32):
        return self.es.enter_context(self.c.nc.sbuf_tensor(name, shape, dt))

    def ps(self, name, shape, dt=F32):
        return self.es.enter_context(self.c.nc.psum_tensor(name, shape, dt))

    def op(self, eng, fn, waits=(), sig=None, amt=1, tag="", drain=False):
        flat = []

        def _fl(w):
            if w is None:
                return
            if isinstance(w, (tuple, list)):
                for x in w:
                    _fl(x)
            else:
                flat.append(w)
        _fl(waits)
        waits = flat
        if drain and sig is None:
            sig = self.c.dsem[eng]
        it = It(eng, fn, waits, sig, amt, tag, drain)
        self.items.append(it)
        return Ev(sig, it) if sig is not None else None

    def dma(self, eng, out, in_, waits=(), tag="", sem=None, **kw):
        if sem is None:
            sem = self.c.qsem[eng]
        return self.op(eng, lambda e: e.dma_start(out=out, in_=in_, **kw),
                       waits=waits, sig=sem, amt=16, tag=tag)

    def emit(self):
        nc = self.c.nc
        for it in self.items:
            if it.sig is not None:
                it.sig.n += it.amt
                it.val = it.sig.n
        with nc.Block() as block:
            for engname in ENGINES:
                items = [it for it in self.items if it.eng == engname]
                if not items:
                    continue

                def make(items, engname=engname):
                    def body(eng):
                        ds = self.c.dsem[engname]

                        def dr(inst):
                            ds.n += 1
                            inst.then_inc(ds.h, 1)
                            eng.wait_ge(ds.h, ds.n)
                            return inst
                        self.c.dr = dr
                        for it in items:
                            for ev in it.waits:
                                if ev is None:
                                    continue
                                eng.wait_ge(ev.sem.h, ev.val)
                            inst = it.fn(eng)
                            if it.sig is not None:
                                inst.then_inc(it.sig.h, it.amt)
                                if it.drain:
                                    eng.wait_ge(it.sig.h, it.val)
                    return body
                getattr(block, engname)(make(items))
        self.es.close()


class Ctx:
    pass


def _barrier(c):
    ph = Ph(c)
    c.bar_n += len(ENGINES)
    tgt = c.bar_n
    for engname in ENGINES:
        drains = [s for s in c.stream_sems if s.eng == engname and s.n > 0]

        def fn(eng, drains=drains):
            for s in drains:
                eng.wait_ge(s.h, s.n)
            return eng.nop()
        it = It(engname, fn, (), c.bar, 1, "bar")
        ph.items.append(it)

        def fnw(eng, tgt=tgt):
            return eng.wait_ge(c.bar.h, tgt)
        ph.items.append(It(engname, fnw, (), None, 1, "barw"))
    ph.emit()



def build_program(core_row, debug=(), sim_safe=False, n_cc=8, max_phase=99, p4_tail=True, p4_y=True, p4_dup=True, p4_b64=True, p4_part='all'):
    nc = bass.Bass("TRN2", target_bir_lowering=False)
    c = Ctx()
    c.nc = nc
    c.es = ExitStack()
    c.bar_n = 0

    def dram(name, shape, dt, kind="Internal"):
        if name in debug and kind == "Internal":
            kind = "ExternalOutput"
        return nc.dram_tensor(name, shape, dt, kind=kind)

    def sem(name):
        return Sem(c.es.enter_context(nc.semaphore(name)))

    def sbuf(name, shape, dt=F32):
        return c.es.enter_context(nc.sbuf_tensor(name, shape, dt))

    def psum(name, shape, dt=F32):
        return c.es.enter_context(nc.psum_tensor(name, shape, dt))

    c.sbuf, c.psum, c.dram = sbuf, psum, dram
    c.sim_safe = sim_safe
    c.IAMT = 1 if sim_safe else 16
    c.n_cc = n_cc
    c.p4_tail = p4_tail
    c.p4_y = p4_y
    c.p4_dup = p4_dup
    c.p4_b64 = p4_b64
    c.p4_part = p4_part

    c.stream_sems = []
    POOL_N = {"sync": 16, "gpsimd": 10, "scalar": 2}
    c.qpool = {}
    for engname, n in POOL_N.items():
        c.qpool[engname] = []
        for i in range(n):
            s = sem(f"q_{engname}_{i}")
            s.eng = engname
            c.qpool[engname].append(s)
            c.stream_sems.append(s)
    c.qcursor = {k: 0 for k in POOL_N}

    def mk_qsem(name, eng):
        i = c.qcursor[eng]
        assert i < len(c.qpool[eng]), f"sem pool exhausted for {eng}"
        c.qcursor[eng] = i + 1
        return c.qpool[eng][i]
    c.mk_qsem = mk_qsem

    def reset_pool():
        c.qcursor = {k: 0 for k in POOL_N}
    c.reset_pool = reset_pool
    c.qsem = {"sync": mk_qsem("qS", "sync"), "gpsimd": mk_qsem("qG", "gpsimd"),
              "scalar": mk_qsem("qA", "scalar")}
    c.qbase = dict(c.qcursor)
    c.sV = sem("sV")
    c.sA = sem("sA")
    c.sT = sem("sT")
    c.sG = sem("sG")
    c.bar = sem("bar")
    c.cs = sem("cs")
    c.dsem = {n: sem("dr_" + n) for n in ENGINES}

    ein = lambda n, s, d=F32: nc.dram_tensor(n, s, d, kind="ExternalInput")
    c.xT = ein("xT", [DM, XNW])
    c.x_tok = ein("x_tok", [T, DM])
    c.WINP = ein("WINP", [64, 128, NJ, 128])
    c.WINPA = ein("WINPA", [128, NJ, 32])
    c.CONVW = ein("CONVW", [128, OXBC * 4])
    c.CONVB = ein("CONVB", [128, OXBC])
    c.ZBF = ein("ZBF", [128, OZ])
    c.DF = ein("DF", [128, NJ])
    c.GWT = ein("GWT", [128, NJ, NE])
    c.WOT = ein("WOT", [NJ, 128, DM])
    c.EUP = ein("EUP", [NE, NUC, 128, NJ, 128])
    c.EDN = ein("EDN", [NE, NIC, 128, DM])
    c.SUP = ein("SUP", [NSUC, 128, NJ, 128])
    c.SDN = ein("SDN", [NSDC, 128, DM])
    c.MSK = ein("MSK", [64, 8])
    c.IDB = ein("IDB", [128, 128], BF16)
    c.IDF = ein("IDF", [128, 128], F32)
    c.LT = ein("LT", [128, 128], F32)
    c.UT = ein("UT", [128, 128], F32)
    c.ND4 = ein("ND4", [128, 512], F32)
    c.I4 = ein("I4", [128, 512], F32)
    c.NEGI = ein("NEGI", [128, 128], F32)
    c.CCOL = ein("CCOL", [128, 8], F32)
    c.CROW = ein("CROW", [1, 256], F32)
    c.CAPT = ein("CAPT", [128, NE], F32)
    c.TRASHM = ein("TRASHM", [128, NE], F32)
    c.NEG1 = ein("NEG1", [128, NE], F32)

    c.OUT = nc.dram_tensor("OUT", [T, DM], F32, kind="ExternalOutput")

    c.xnF = dram("xnF", [DM, XNW], BF16)
    c.xn_tok = dram("xn_tok", [T, DM], BF16)
    c.idxcw = dram("idxcw", [NE * CAPE + 32 * 2048, 2], F32)
    c.xF32 = dram("xF32", [DM, T], F32)
    c.xFbf = dram("xFbf", [DM, T], BF16)
    c.Bf = dram("Bf", [DM, T], BF16)
    c.Cf = dram("Cf", [DM, T], BF16)
    c.zs = dram("zs", [DM, T], F32)
    c.Sdr = dram("Sdr", [NCH * 64, DM], F32)
    c.cc_in = dram("cc_in", [2080, 64], F32)
    c.cc_out = dram("cc_out", [8 * 2080, 64], F32)
    c.gacc = dram("gacc", [T, DM], F32)
    c.s_tok = dram("s_tok", [T, DM], F32)
    c.aFs = dram("aFs", [SID, T], BF16)
    c.dbg = dram("dbg", [128, 256], F32)
    c.dbg6 = dram("dbg6", [128, 2048], F32)
    c.dbg6b = dram("dbg6b", [128, 4096], BF16)
    c.dbg7 = dram("dbg7", [128, 2560], F32)

    c.gwT = sbuf("gwT", [128, NJ, NE], F32)
    c.convw = sbuf("convw", [128, OXBC * 4], F32)
    c.convb = sbuf("convb", [128, OXBC], F32)
    c.zbf = sbuf("zbf", [128, OZ], F32)
    c.df = sbuf("dfc", [128, NJ], F32)
    c.idb = sbuf("idb", [128, 128], BF16)
    c.idf = sbuf("idf", [128, 128], F32)
    c.lt = sbuf("lt", [128, 128], F32)
    c.ut = sbuf("ut", [128, 128], F32)
    c.nd4 = sbuf("nd4", [128, 512], F32)
    c.i4 = sbuf("i4", [128, 512], F32)
    c.negi = sbuf("negi", [128, 128], F32)
    c.ccol = sbuf("ccol", [128, 8], F32)
    c.crow = sbuf("crow", [1, 256], F32)
    c.capt = sbuf("capt", [128, NE], F32)
    c.trashm = sbuf("trashm", [128, NE], F32)
    c.neg1 = sbuf("neg1", [128, NE], F32)
    c.mes = ExitStack()
    msb = lambda n, s, d=F32: c.mes.enter_context(nc.sbuf_tensor(n, s, d))
    c.ndtF = msb("ndtF", [64, DM])
    c.cumTok = msb("cumTok", [128, NCH * 32])
    c.cdr = msb("cdr", [1, NCH * 32])
    c.carry = msb("carry", [64, DM])
    c.carrybf = msb("carrybf", [64, DM], BF16)
    c.drow = msb("drow", [1, 32])
    c.msk = msb("msk", [64, 8])

    phases = [p0, p1, p2, p3a, p6, cccomb, p3b4]
    for pi, pf in enumerate(phases):
        if pi > max_phase:
            c.mes.close()
            c.es.close()
            return nc
        pf(c)
        _barrier(c)
    c.mes.close()
    if max_phase >= 7:
        p7a(c)
        _barrier(c)
    if max_phase >= 8:
        p8(c)
        _barrier(c)

    c.es.close()
    return nc


def p0(c):
    ph = Ph(c)
    loads = [
        (c.gwT[:, :, :], c.GWT.ap()),
        (c.convw[:, :], c.CONVW.ap()),
        (c.convb[:, :], c.CONVB.ap()),
        (c.zbf[:, :], c.ZBF.ap()),
        (c.df[:, :], c.DF.ap()),
        (c.msk[:, :], c.MSK.ap()),
        (c.idb[:, :], c.IDB.ap()),
        (c.idf[:, :], c.IDF.ap()),
        (c.lt[:, :], c.LT.ap()),
        (c.ut[:, :], c.UT.ap()),
        (c.nd4[:, :], c.ND4.ap()),
        (c.i4[:, :], c.I4.ap()),
        (c.negi[:, :], c.NEGI.ap()),
        (c.ccol[:, :], c.CCOL.ap()),
        (c.crow[:, :], c.CROW.ap()),
        (c.capt[:, :], c.CAPT.ap()),
        (c.trashm[:, :], c.TRASHM.ap()),
        (c.neg1[:, :], c.NEG1.ap()),
    ]
    for out, in_ in loads:
        ph.dma("sync", out, in_)
    zero2k = ph.sb("zero2k", [128, DM], F32)
    ez = ph.op("vector", lambda e: e.memset(zero2k[:, :], 0.0), sig=c.sV)
    ph.dma("gpsimd",
           c.idxcw.ap()[0:NE * CAPE, :].rearrange("(p a) t -> p (a t)", p=128),
           zero2k[:, 0:2 * NE * CAPE // 128], waits=(ez,))
    for r in range(4):
        ph.dma("gpsimd",
               c.gacc.ap()[r * 1024:(r + 1) * 1024, :]
               .rearrange("(a p) d -> p a d", p=128),
               zero2k[:, :].unsqueeze(1).broadcast_to([128, 8, DM]),
               waits=(ez,))
    ph.emit()


def p1(c):
    ph = Ph(c)
    qXT = [c.mk_qsem(f"qXT{i}", "sync") for i in range(2)]
    qXNF = [c.mk_qsem(f"qXNF{i}", "sync") for i in range(2)]
    qTOK = [c.mk_qsem(f"qTOK{i}", "gpsimd") for i in range(2)]
    qSC = c.mk_qsem("qSC", "gpsimd")
    xt = [ph.sb(f"p1xt{i}", [128, NJ, 128], F32) for i in range(2)]
    sq = [ph.sb(f"p1sq{i}", [128, NJ, 128], F32) for i in range(2)]
    xn32 = [ph.sb(f"p1xn32{i}", [128, NJ, 128], F32) for i in range(2)]
    xnbf = [ph.sb(f"p1xnbf{i}", [128, NJ, 128], BF16) for i in range(2)]
    stat = [ph.sb(f"p1stat{i}", [1, 384], F32) for i in range(2)]
    toksb = [ph.sb(f"p1tok{i}", [128, DM], BF16) for i in range(2)]
    gsc = ph.sb("p1gsc", [128, NE], F32)
    gsc2 = ph.sb("p1gsc2", [128, NE], F32)
    gms = ph.sb("p1gms", [128, NE], F32)
    gt1 = ph.sb("p1gt1", [128, NE], F32)
    gt2 = ph.sb("p1gt2", [128, NE], F32)
    grow = ph.sb("p1grow", [128, 10], F32)
    gmx = ph.sb("p1gmx", [128, 16], F32)
    gu8 = ph.sb("p1gu8", [128, NE], U8)
    gmf = ph.sb("p1gmf", [128, NE], F32)
    gmt = ph.sb("p1gmt", [128, NE], F32)
    gtr = ph.sb("p1gtr", [128, NE], F32)
    cwsb = ph.sb("p1cw", [128, NE], F32)
    selsb = ph.sb("p1sel", [128, NE], F32)
    r1 = ph.sb("p1r1", [128, NE], F32)
    possb = ph.sb("p1pos", [128, NE], F32)
    posi = ph.sb("p1posi", [128, NE], I32)
    scv = ph.sb("p1scv", [128, 2 * NE], F32)
    tokid = ph.sb("p1tokid", [128, 1], F32)
    baserow = ph.sb("p1base", [1, NE], F32)
    dbgsb = ph.sb("p1dbg", [128, 256], F32)
    pstat = [ph.ps(f"p1pstat{i}", [128, 512], F32) for i in range(2)]
    pbc = [ph.ps(f"p1pbc{i}", [128, 512], F32) for i in range(2)]
    pgl = ph.ps("p1pgl", [128, 512], F32)
    pms = ph.ps("p1pms", [128, 512], F32)
    ptpa = ph.ps("p1ptpa", [128, 1024], BF16)
    ptpb = ph.ps("p1ptpb", [128, 1024], BF16)

    def init0(e):
        c.dr(e.memset(dbgsb[:, :], 0.0))
        return e.tensor_copy(baserow[:, :], c.crow[0:1, 128:128 + NE])
    prev_posf = ph.op("vector", init0, sig=c.sV)
    ev_stats = [None, None]
    ev_xn = [None, None]
    ev_gmm = [None, None]
    ev_tp = [None, None]
    ev_xnFst = [None, None]
    prev_e2 = None
    prev_e3 = None
    prev_eva = None
    prev_evb = None
    prev_store = [None, None]
    prev_scat = None
    ld = [None, None]
    NT1 = T // 128
    for g in range(NT1 + 1):
        mini = (g == NT1)
        W = 128
        ts = g * 128
        par = g % 2
        ld[par] = ph.dma(
            "sync", xt[par][:, :, 0:W],
            c.xT.ap()[:, ts:ts + W].rearrange("(j p) t -> p j t", p=128),
            waits=(ev_xn[par],) if g >= 2 else (), tag=f"p1xt{g}", sem=qXT[par])
        esq = ph.op("vector", lambda e, par=par, W=W: e.tensor_tensor(
            sq[par][:, :, 0:W], xt[par][:, :, 0:W], xt[par][:, :, 0:W], ALU.mult),
            waits=(ld[par],), sig=c.sV)

        def sums(e, par=par, W=W):
            for j in range(NJ):
                e.matmul(pstat[par][0:1, 0:W], c.ccol[:, 0:1], xt[par][:, j, 0:W],
                         start=(j == 0), stop=(j == NJ - 1))
            for j in range(NJ):
                i_ = e.matmul(pstat[par][0:1, 128:128 + W], c.ccol[:, 0:1],
                              sq[par][:, j, 0:W], start=(j == 0), stop=(j == NJ - 1))
            return i_
        esums = ph.op("tensor", sums, waits=(esq, ev_stats[par]), sig=c.sT)

        def stats(e, par=par, W=W):
            st_ = stat[par]
            c.dr(e.tensor_scalar(st_[:, 0:W], pstat[par][0:1, 0:W], 1.0 / DM,
                                 None, ALU.mult))
            c.dr(e.tensor_scalar(st_[:, 128:128 + W], pstat[par][0:1, 128:128 + W],
                                 1.0 / DM, 1e-5, ALU.mult, ALU.add))
            c.dr(e.tensor_tensor(st_[:, 256:256 + W], st_[:, 0:W], st_[:, 0:W],
                                 ALU.mult))
            return e.tensor_tensor(st_[:, 128:128 + W], st_[:, 128:128 + W],
                                   st_[:, 256:256 + W], ALU.subtract)
        ev_stats[par] = ph.op("vector", stats, waits=(esums,), sig=c.sV)

        def rstd(e, par=par, W=W):
            st_ = stat[par]
            c.dr(e.activation(st_[:, 256:256 + W], st_[:, 128:128 + W], AF.Ln))
            return e.activation(st_[:, 256:256 + W], st_[:, 256:256 + W],
                                AF.Exp, scale=-0.5)
        erstd = ph.op("scalar", rstd, waits=(ev_stats[par],), sig=c.sA)

        def bcast(e, par=par, W=W):
            st_ = stat[par]
            e.matmul(pbc[par][:, 0:W], c.crow[0:1, 0:128], st_[:, 0:W],
                     start=True, stop=True)
            return e.matmul(pbc[par][:, 128:128 + W], c.crow[0:1, 0:128],
                            st_[:, 256:256 + W], start=True, stop=True)
        ebc = ph.op("tensor", bcast, waits=(erstd, ev_xn[par]), sig=c.sT)

        def xn(e, par=par, W=W):
            c.dr(e.tensor_tensor(xn32[par][:, :, 0:W], xt[par][:, :, 0:W],
                                 pbc[par][:, 0:W].unsqueeze(1)
                                 .broadcast_to([128, NJ, W]), ALU.subtract))
            c.dr(e.tensor_tensor(xn32[par][:, :, 0:W], xn32[par][:, :, 0:W],
                                 pbc[par][:, 128:128 + W].unsqueeze(1)
                                 .broadcast_to([128, NJ, W]), ALU.mult))
            return e.tensor_copy(xnbf[par][:, :, 0:W], xn32[par][:, :, 0:W])
        ev_xn[par] = ph.op("vector", xn,
                           waits=(ebc, ev_gmm[par], ev_tp[par], ev_xnFst[par]),
                           sig=c.sV)
        ev_xnFst[par] = ph.dma(
            "sync", c.xnF.ap()[:, ts:ts + W].rearrange("(j p) t -> p j t", p=128),
            xnbf[par][:, :, 0:W], waits=(ev_xn[par],), tag=f"p1xnF{g}", sem=qXNF[par])
        if mini:
            break

        def gmm(e, par=par):
            for j in range(NJ):
                i_ = e.matmul(pgl[:, 0:NE], xn32[par][:, j, :], c.gwT[:, j, :],
                              start=(j == 0), stop=(j == NJ - 1))
            return i_
        ev_gmm[par] = ph.op("tensor", gmm, waits=(ev_xn[par], prev_e2), sig=c.sT)

        def sm1(e):
            c.dr(e.tensor_copy(gsc2[:, :], pgl[:, 0:NE]))
            return e.reduce_max(grow[:, 1:2], gsc2[:, :], axis=AX.X, negate=True)
        e1 = ph.op("vector", sm1, waits=(ev_gmm[par], prev_e3), sig=c.sV)
        prev_e2 = ph.op("scalar", lambda e: e.activation(
            gsc[:, :], gsc2[:, :], AF.Exp, bias=grow[:, 1:2]),
            waits=(e1,), sig=c.sA)

        def gat(e):
            d = c.dr
            d(e.reduce_sum(grow[:, 2:3], gsc[:, :], axis=AX.X))
            d(e.reciprocal(grow[:, 3:4], grow[:, 2:3]))
            d(e.tensor_scalar(gsc[:, :], gsc[:, :], grow[:, 3:4], None, ALU.mult))
            d(e.tensor_tensor(gmx[:, 0:8], gsc[:, 0:16:2], gsc[:, 1:16:2], ALU.max))
            d(e.tensor_tensor(gmx[:, 8:12], gmx[:, 0:8:2], gmx[:, 1:8:2], ALU.max))
            d(e.tensor_tensor(gmx[:, 12:14], gmx[:, 8:12:2], gmx[:, 9:12:2], ALU.max))
            d(e.tensor_tensor(gmx[:, 14:16], gmx[:, 8:12:2], gmx[:, 9:12:2], ALU.min))
            d(e.tensor_tensor(grow[:, 4:5], gmx[:, 12:13], gmx[:, 13:14], ALU.min))
            d(e.tensor_tensor(grow[:, 5:6], gmx[:, 14:15], gmx[:, 15:16], ALU.max))
            d(e.tensor_tensor(grow[:, 6:7], grow[:, 4:5], grow[:, 5:6], ALU.max))
            d(e.tensor_scalar(gmx[:, 0:4], gmx[:, 8:12], grow[:, 6:7], None,
                              ALU.is_ge))
            d(e.tensor_tensor(gms[:, :].rearrange("p (g r) -> p g r", g=4),
                              gsc[:, :].rearrange("p (g r) -> p g r", g=4),
                              gmx[:, 0:4].unsqueeze(2).broadcast_to([128, 4, 4]),
                              ALU.mult))
            d(e.tensor_copy(gt1[:, :], gms[:, :]))
            for r in range(3):
                src_, dst_ = (gt1, gt2) if r % 2 == 0 else (gt2, gt1)
                d(e.reduce_max(grow[:, 7:8], src_[:, :], axis=AX.X))
                d(e.tensor_scalar(gmf[:, :], src_[:, :], grow[:, 7:8], None,
                                  ALU.is_ge))
                d(e.tensor_scalar(gmt[:, :], src_[:, :], 1.0, None, ALU.add))
                d(e.tensor_tensor(gmt[:, :], gmt[:, :], gmf[:, :], ALU.mult))
                d(e.tensor_tensor(dst_[:, :], src_[:, :], gmt[:, :], ALU.subtract))
            d(e.reduce_max(grow[:, 7:8], gt2[:, :], axis=AX.X))
            d(e.tensor_scalar(selsb[:, :], gms[:, :], grow[:, 7:8], None,
                              ALU.is_ge))
            d(e.tensor_tensor(cwsb[:, :], gms[:, :], selsb[:, :], ALU.mult))
            d(e.reduce_sum(grow[:, 8:9], cwsb[:, :], axis=AX.X))
            d(e.tensor_scalar(grow[:, 8:9], grow[:, 8:9], 1e-20, None, ALU.add))
            d(e.reciprocal(grow[:, 9:10], grow[:, 8:9]))
            return e.tensor_scalar(cwsb[:, :], cwsb[:, :], grow[:, 9:10], None,
                                   ALU.mult)
        prev_e3 = ph.op("vector", gat, waits=(prev_e2,), sig=c.sV, drain=True)

        def rmm(e):
            e.matmul(pms[:, 0:NE], c.ut[:, :], selsb[:, :], start=True, stop=True)
            e.matmul(pms[0:1, 32:32 + NE], c.ccol[:, 0:1], selsb[:, :],
                     start=True, stop=True)
            return e.matmul(pms[:, 64:64 + NE], c.crow[0:1, 0:128],
                            baserow[0:1, :], start=True, stop=True)
        e4 = ph.op("tensor", rmm, waits=(prev_e3, prev_posf), sig=c.sT)

        def posf(e, g=g):
            d = c.dr
            d(e.tensor_copy(r1[:, :], pms[:, 0:NE]))
            d(e.tensor_tensor(possb[:, :], r1[:, :], pms[:, 64:64 + NE], ALU.add))
            d(e.tensor_tensor(possb[:, :], possb[:, :], c.capt[:, :], ALU.min))
            d(e.tensor_scalar(gmf[:, :], cwsb[:, :], 0.0, None, ALU.is_gt))
            d(e.tensor_scalar(gtr[:, :], c.trashm[:, :], float(g * 2048), None,
                              ALU.add))
            d(e.tensor_tensor(gmt[:, :], possb[:, :], gtr[:, :], ALU.subtract))
            d(e.tensor_tensor(gmt[:, :], gmt[:, :], gmf[:, :], ALU.mult))
            d(e.tensor_tensor(possb[:, :], gmt[:, :], gtr[:, :], ALU.add))
            d(e.tensor_copy(posi[:, :], possb[:, :]))
            d(e.tensor_scalar(tokid[:, :], c.ccol[:, 2:3], float(g * 128), None,
                              ALU.add))
            for ee in range(NE):
                e.tensor_copy(scv[:, 2 * ee:2 * ee + 1], tokid[:, 0:1])
                d(e.tensor_copy(scv[:, 2 * ee + 1:2 * ee + 2], cwsb[:, ee:ee + 1]))
            return e.tensor_tensor(baserow[0:1, :], baserow[0:1, :],
                                   pms[0:1, 32:32 + NE], ALU.add)
        prev_posf = ph.op("vector", posf, waits=(e4, prev_scat), sig=c.sV, drain=True)
        if g == 0:
            def dumpdbg(e):
                c.dr(e.tensor_copy(dbgsb[:, 0:16], gsc[:, :]))
                c.dr(e.tensor_copy(dbgsb[:, 16:32], gms[:, :]))
                c.dr(e.tensor_copy(dbgsb[:, 32:48], selsb[:, :]))
                c.dr(e.tensor_copy(dbgsb[:, 48:64], cwsb[:, :]))
                c.dr(e.tensor_copy(dbgsb[:, 64:80], possb[:, :]))
                c.dr(e.tensor_copy(dbgsb[:, 80:96], gmx[:, :]))
                return e.tensor_copy(dbgsb[:, 97:106], grow[:, 1:10])
            ed = ph.op("vector", dumpdbg, waits=(prev_posf,), sig=c.sV)
            ph.dma("gpsimd", c.dbg.ap(), dbgsb[:, :], waits=(ed,))
        for ee in range(NE):
            w = [prev_posf] if ee == 0 else []
            if c.sim_safe:
                w.append(evs if ee > 0 else prev_scat)
            evs = ph.op("gpsimd", lambda e, ee=ee: e.indirect_dma_start(
                out=c.idxcw.ap(),
                out_offset=IndirectOffsetOnAxis(posi[:, ee:ee + 1], axis=0),
                in_=scv[:, 2 * ee:2 * ee + 2], in_offset=None),
                waits=w, sig=qSC, amt=16)
        prev_scat = evs
        tb = toksb[par]

        def tpa(e, par=par):
            for jj in range(8):
                i_ = e.transpose(ptpa[:, jj * 128:jj * 128 + 128],
                                 xnbf[par][:, jj, :], c.idb[:, :])
            return i_
        etpa = ph.op("tensor", tpa, waits=(ev_xn[par], prev_eva), sig=c.sT)
        prev_eva = ph.op("scalar", lambda e, tb=tb: e.activation(
            tb[:, 0:1024], ptpa[:, :], AF.Copy),
            waits=(etpa, prev_store[par]), sig=c.sA)

        def tpb(e, par=par):
            for jj in range(8, NJ):
                i_ = e.transpose(ptpb[:, (jj - 8) * 128:(jj - 8) * 128 + 128],
                                 xnbf[par][:, jj, :], c.idb[:, :])
            return i_
        etpb = ph.op("tensor", tpb, waits=(ev_xn[par], prev_evb), sig=c.sT)
        prev_evb = ph.op("scalar", lambda e, tb=tb: e.activation(
            tb[:, 1024:2048], ptpb[:, :], AF.Copy),
            waits=(etpb, prev_store[par]), sig=c.sA)
        ev_tp[par] = prev_evb
        prev_store[par] = ph.dma(
            "gpsimd", c.xn_tok.ap()[g * 128:(g + 1) * 128, :], tb[:, :],
            waits=(prev_eva, prev_evb), tag=f"p1tok{g}", sem=qTOK[par])
    ph.emit()


def p2(c):
    ph = Ph(c)
    NW = 8
    qW = [c.mk_qsem(f"p2qW{i}", "gpsimd") for i in range(NW)]
    qXN = [c.mk_qsem(f"p2qXN{i}", "sync") for i in range(2)]
    qXH = c.mk_qsem("p2qXH", "sync")
    qXF = [c.mk_qsem(f"p2qXF{i}", "sync") for i in range(2)]
    qXB = [c.mk_qsem(f"p2qXB{i}", "sync") for i in range(2)]
    qBF = [c.mk_qsem(f"p2qBF{i}", "sync") for i in range(2)]
    qCF = [c.mk_qsem(f"p2qCF{i}", "sync") for i in range(2)]
    qZS = [c.mk_qsem(f"p2qZS{i}", "sync") for i in range(2)]

    GROUPS = [list(range(0, 22)), list(range(22, 44)), list(range(44, 65))]
    wsb = [ph.sb(f"p2w{i}", [128, NJ, 128], BF16) for i in range(22)]
    xnt = [ph.sb(f"p2xn{i}", [128, NJ, 512], BF16) for i in range(2)]
    xnh = ph.sb("p2xnh", [128, NJ, HALO], BF16)
    wk = [ph.sb(f"p2wk{i}", [128, 515], F32) for i in range(2)]
    cv1 = [ph.sb(f"p2cv1{i}", [128, 512], F32) for i in range(2)]
    cv2 = [ph.sb(f"p2cv2{i}", [128, 512], F32) for i in range(2)]
    cvb = [ph.sb(f"p2cvb{i}", [128, 512], BF16) for i in range(2)]
    tails = ph.sb("p2tails", [128, OXBC * 4], F32)
    spa = ph.sb("p2spa", [32, 512 * 3], F32)
    pP = [ph.ps(f"p2pP{i}", [128, 512], F32) for i in range(2)]

    ev_w = {}
    ev_mm = [None, None]
    ev_cv1 = [None, None]
    ev_cv2 = [None, None]
    ev_cvb = [None, None]
    ev_zs = [None, None]
    ev_xnl = [None, None]
    ev_tail = {}
    pcount = [0]

    last_use = [None] * 22
    last_q = [None] * NW

    def wload(ci):
        src_ap = (c.WINPA.ap() if ci == 64 else c.WINP.ap()[ci])
        ev_w[ci] = ph.dma("gpsimd", wsb[ci % 22][:, :, 0:(32 if ci == 64 else 128)],
                          src_ap, sem=qW[ci % NW], tag=f"w{ci}",
                          waits=(last_use[ci % 22], last_q[ci % NW]))
        last_q[ci % NW] = ev_w[ci]

    for gi, grp in enumerate(GROUPS):
        for ci in grp:
            wload(ci)
        if gi == 0:
            ev_hl = ph.dma(
                "sync", xnh[:, :, :],
                c.xnF.ap()[:, T:T + HALO].rearrange("(j p) t -> p j t", p=128),
                sem=qXH, tag="xnh")
        for t in range(NTT):
            par = t % 2
            ev_xnl[par] = ph.dma(
                "sync", xnt[par][:, :, :],
                c.xnF.ap()[:, t * 512:(t + 1) * 512]
                .rearrange("(j p) t -> p j t", p=128),
                sem=qXN[par], tag=f"p2xn{gi}_{t}",
                waits=(ev_mm[par],) if ev_mm[par] is not None else ())
            for ci in grp:
                wi = ci % 22
                is_x = ci < 16
                is_b = 16 <= ci < 32
                is_c = 32 <= ci < 48
                is_z = 48 <= ci < 64
                is_a = ci == 64
                is_conv = ci < 48
                pp = pcount[0] % 2
                pcount[0] += 1
                O = 32 if is_a else 128

                if is_conv and t == 0:
                    def hmm(e, wi=wi, O=O):
                        for j in range(NJ):
                            i_ = e.matmul(pP[0][0:O, 0:HALO], wsb[wi][:, j, 0:O],
                                          xnh[:, j, :], start=(j == 0),
                                          stop=(j == NJ - 1))
                        return i_
                    eh = ph.op("tensor", hmm,
                               waits=(ev_w[ci], ev_hl, ev_mm[0]), sig=c.sT)
                    ev_mm[0] = ph.op("vector", lambda e, ci=ci: e.tensor_copy(
                        tails[:, ci * 4:ci * 4 + HALO], pP[0][:, 0:HALO]),
                        waits=(eh,), sig=c.sV, drain=True)
                    ev_tail[ci] = ev_mm[0]
                    pcount[0] += 1

                def mm(e, wi=wi, par=par, pp=pp, O=O):
                    for j in range(NJ):
                        i_ = e.matmul(pP[pp][0:O, :], wsb[wi][:, j, 0:O],
                                      xnt[par][:, j, :], start=(j == 0),
                                      stop=(j == NJ - 1))
                    return i_
                emm = ph.op("tensor", mm,
                            waits=(ev_w[ci], ev_xnl[par], ev_mm[pp]), sig=c.sT)
                last_use[wi] = emm

                if is_conv:
                    def evac(e, pp=pp, ci=ci):
                        d = c.dr
                        d(e.tensor_copy(wk[pp][:, 0:HALO],
                                        tails[:, ci * 4:ci * 4 + HALO]))
                        d(e.tensor_copy(wk[pp][:, HALO:515], pP[pp][:, :]))
                        return e.tensor_copy(tails[:, ci * 4:ci * 4 + HALO],
                                             wk[pp][:, 512:515])
                    ev_ev = ph.op("vector", evac,
                                  waits=(emm, ev_tail.get(ci), ev_cv2[pp]),
                                  sig=c.sV, drain=True)
                    ev_tail[ci] = ev_ev
                    ev_mm[pp] = ev_ev
                    e0 = ph.op("vector", lambda e, pp=pp, ci=ci: e.tensor_scalar(
                        cv1[pp][:, :], wk[pp][:, 0:512],
                        c.convw[:, 4 * ci:4 * ci + 1], c.convb[:, ci:ci + 1],
                        ALU.mult, ALU.add),
                        waits=(ev_ev, ev_cv1[pp]), sig=c.sV)
                    e1 = ph.op("vector", lambda e, pp=pp, ci=ci: e.scalar_tensor_tensor(
                        cv2[pp][:, :], wk[pp][:, 1:513],
                        c.convw[:, 4 * ci + 1:4 * ci + 2], cv1[pp][:, :],
                        ALU.mult, ALU.add), waits=(e0,), sig=c.sV)
                    e2 = ph.op("vector", lambda e, pp=pp, ci=ci: e.scalar_tensor_tensor(
                        cv1[pp][:, :], wk[pp][:, 2:514],
                        c.convw[:, 4 * ci + 2:4 * ci + 3], cv2[pp][:, :],
                        ALU.mult, ALU.add), waits=(e1,), sig=c.sV)
                    ev_cv2[pp] = e2
                    e3 = ph.op("vector", lambda e, pp=pp, ci=ci: e.scalar_tensor_tensor(
                        cv2[pp][:, :], wk[pp][:, 3:515],
                        c.convw[:, 4 * ci + 3:4 * ci + 4], cv1[pp][:, :],
                        ALU.mult, ALU.add), waits=(e2,), sig=c.sV)
                    ev_cv1[pp] = e3
                    if is_x:
                        row = ci
                        ev_st = ph.dma("sync",
                                       c.xF32.ap()[row * 128:(row + 1) * 128,
                                                   t * 512:(t + 1) * 512],
                                       cv2[pp][:, :], waits=(e3,), sem=qXF[pp],
                                       tag=f"xf{ci}_{t}")
                        eb = ph.op("vector", lambda e, pp=pp: e.tensor_copy(
                            cvb[pp][:, :], cv2[pp][:, :]),
                            waits=(e3, ev_cvb[pp]), sig=c.sV)
                        ev_cvb[pp] = ph.dma(
                            "sync", c.xFbf.ap()[row * 128:(row + 1) * 128,
                                                t * 512:(t + 1) * 512],
                            cvb[pp][:, :], waits=(eb,), sem=qXB[pp],
                            tag=f"xb{ci}_{t}")
                        ev_cv2[pp] = (eb, ev_st)
                    else:
                        row = ci - 16 if is_b else ci - 32
                        q = qBF if is_b else qCF
                        dst = c.Bf if is_b else c.Cf
                        eb = ph.op("vector", lambda e, pp=pp: e.tensor_copy(
                            cvb[pp][:, :], cv2[pp][:, :]),
                            waits=(e3, ev_cvb[pp]), sig=c.sV)
                        ev_cv2[pp] = eb
                        ev_cvb[pp] = ph.dma(
                            "sync", dst.ap()[row * 128:(row + 1) * 128,
                                             t * 512:(t + 1) * 512],
                            cvb[pp][:, :], waits=(eb,), sem=q[pp],
                            tag=f"bc{ci}_{t}")
                elif is_z:
                    zi = ci - 48
                    def zev(e, pp=pp, zi=zi):
                        return e.activation(cv2[pp][:, :], pP[pp][:, :], AF.Silu,
                                            bias=c.zbf[:, zi:zi + 1])
                    ez = ph.op("scalar", zev, waits=(emm, ev_cv2[pp]), sig=c.sA)
                    ev_mm[pp] = ez
                    ev_cv2[pp] = ph.dma(
                        "sync", c.zs.ap()[zi * 128:(zi + 1) * 128,
                                          t * 512:(t + 1) * 512],
                        cv2[pp][:, :], waits=(ez,), sem=qZS[pp],
                        tag=f"zs{zi}_{t}")
                else:
                    def aev(e, pp=pp):
                        d = c.dr
                        d(e.activation(spa[:, 0:512], pP[pp][0:32, :], AF.Abs))
                        d(e.activation(spa[:, 512:1024], spa[:, 0:512], AF.Exp,
                                       scale=-1.0))
                        return e.activation(spa[:, 1024:1536], spa[:, 512:1024],
                                            AF.Ln, bias=1.0)
                    ea = ph.op("scalar", aev, waits=(emm,), sig=c.sA)

                    def andt(e, pp=pp, t=t):
                        d = c.dr
                        d(e.tensor_scalar(spa[:, 0:512], pP[pp][0:32, :], 0.0,
                                          None, ALU.max))
                        for k in range(4):
                            cc = t * 4 + k
                            i_ = e.scalar_tensor_tensor(
                                c.ndtF[32 * (cc % 2):32 * (cc % 2) + 32,
                                       (cc // 2) * 128:(cc // 2) * 128 + 128],
                                spa[:, k * 128:(k + 1) * 128], -1.0,
                                spa[:, 1024 + k * 128:1024 + (k + 1) * 128],
                                ALU.mult, ALU.subtract)
                            if k < 3:
                                d(i_)
                        return i_
                    en = ph.op("vector", andt, waits=(ea,), sig=c.sV, drain=True)
                    ev_mm[pp] = en
    ph.emit()


def p3a(c):
    ph = Ph(c)
    qXT = [c.mk_qsem(f"p3qXT{i}", "sync") for i in range(2)]
    qBT = [c.mk_qsem(f"p3qBT{i}", "sync") for i in range(2)]
    qSS = [c.mk_qsem(f"p3qSS{i}", "sync") for i in range(2)]
    qCP = c.mk_qsem("p3qCP", "gpsimd")

    xtok = [ph.sb(f"p3xt{i}", [128, DM], BF16) for i in range(2)]
    btok = [ph.sb(f"p3bt{i}", [128, DM], BF16) for i in range(2)]
    xsd = [ph.sb(f"p3xsd{i}", [128, DM], BF16) for i in range(2)]
    ssb = [ph.sb(f"p3ssb{i}", [64, DM], F32) for i in range(2)]
    cumf = ph.sb("p3cumf", [32, 128], F32)
    sdt = ph.sb("p3sdt", [128, 32], F32)
    acd = ph.sb("p3acd", [1, 32], F32)
    cdc = ph.sb("p3cdc", [64, 32], F32)
    t1c = ph.sb("p3t1c", [64, DM], F32)
    zcc = ph.sb("p3zcc", [128, 2048], F32)
    pS = ph.ps("p3pS", [64, DM], F32)
    pM = ph.ps("p3pM", [128, 512], F32)

    einit = ph.op("vector", lambda e: (
        c.dr(e.memset(c.carry[:, :], 0.0)),
        c.dr(e.memset(zcc[:, :], 0.0)),
        e.memset(c.drow[:, :], 1.0))[-1], sig=c.sV, drain=True)
    ph.dma("gpsimd", c.cc_in.ap()[2048:2080, :], zcc[0:32, 0:64],
           waits=(einit,), tag="cciz")
    ccov = c.cc_out.ap().rearrange("(p a) d -> p (a d)", p=128)
    for zi in range(5):
        zw = min(2048, 8320 - zi * 2048)
        ph.dma("gpsimd", ccov[:, zi * 2048:zi * 2048 + zw], zcc[:, 0:zw],
               waits=(einit,), tag=f"ccz{zi}")

    ev_xt = [None, None]
    ev_bt = [None, None]
    ev_xsd_used = [None, None]
    ev_ssb_used = [None, None]
    ev_last = None
    ev_smm = None
    ev_tc = None
    for cc in range(NCH):
        par = cc % 2
        base = 32 * (cc % 2)
        col = (cc // 2) * 128
        ev_xt[par] = ph.dma(
            "sync", xtok[par][:, :],
            c.xFbf.ap()[:, cc * 128:(cc + 1) * 128], transpose=True,
            sem=qXT[par], waits=(ev_xsd_used[par],), tag=f"p3x{cc}")
        ev_bt[par] = ph.dma(
            "sync", btok[par][:, :],
            c.Bf.ap()[:, cc * 128:(cc + 1) * 128], transpose=True,
            sem=qBT[par], waits=(ev_smm,) if cc >= 2 else (), tag=f"p3b{cc}")

        ecf = ph.op("vector", lambda e, base=base, col=col: e.tensor_tensor_scan(
            cumf[0:32, :], c.ndtF[base:base + 32, col:col + 128],
            c.ndtF[base:base + 32, col:col + 128], 0.0, ALU.add, ALU.bypass),
            waits=(ev_tc,), sig=c.sV, drain=True)
        etc = ph.op("tensor", lambda e: e.transpose(pM[:, 0:32], cumf[0:32, :],
                                                    c.idf[0:32, 0:32]),
                    waits=(ecf, ev_last), sig=c.sT)
        ev_tc = etc
        ect = ph.op("vector", lambda e, cc=cc: e.tensor_copy(
            c.cumTok[:, cc * 32:(cc + 1) * 32], pM[:, 0:32]),
            waits=(etc,), sig=c.sV, drain=True)
        erow = ph.op("tensor", lambda e, cc=cc: e.matmul(
            pM[0:1, 64:96], c.ccol[:, 1:2], c.cumTok[:, cc * 32:(cc + 1) * 32],
            start=True, stop=True), waits=(ect,), sig=c.sT)
        ecr = ph.op("vector", lambda e, cc=cc: e.tensor_copy(
            c.cdr[0:1, cc * 32:(cc + 1) * 32], pM[0:1, 64:96]),
            waits=(erow,), sig=c.sV, drain=True)
        ebc = ph.op("tensor", lambda e, cc=cc: e.matmul(
            pM[:, 128:160], c.crow[0:1, 0:128],
            c.cdr[0:1, cc * 32:(cc + 1) * 32], start=True, stop=True),
            waits=(ecr,), sig=c.sT)
        esd = ph.op("vector", lambda e, cc=cc: e.tensor_tensor(
            sdt[:, :], pM[:, 128:160], c.cumTok[:, cc * 32:(cc + 1) * 32],
            ALU.subtract), waits=(ebc,), sig=c.sV, drain=True)
        esd2 = ph.op("scalar", lambda e: e.activation(sdt[:, :], sdt[:, :],
                                                      AF.Exp),
                     waits=(esd,), sig=c.sA)
        exsd = ph.op("vector", lambda e, par=par: e.tensor_tensor(
            xsd[par][:, :].rearrange("p (h q) -> p h q", h=N_HEADS),
            xtok[par][:, :].rearrange("p (h q) -> p h q", h=N_HEADS),
            sdt[:, :].unsqueeze(2).broadcast_to([128, N_HEADS, DP]),
            ALU.mult), waits=(esd2, ev_xt[par]), sig=c.sV, drain=True)
        def smm(e, par=par):
            for h in range(N_HEADS):
                i_ = e.matmul(pS[:, h * DP:(h + 1) * DP],
                              btok[par][:, h * DS:(h + 1) * DS],
                              xsd[par][:, h * DP:(h + 1) * DP],
                              start=True, stop=True)
            return i_
        ev_smm = ph.op("tensor", smm,
                       waits=(exsd, ev_bt[par], ev_ssb_used[par]), sig=c.sT)
        ev_xsd_used[par] = ev_smm
        esb = ph.op("vector", lambda e, par=par: e.tensor_copy(
            ssb[par][:, :], pS[:, :]), waits=(ev_smm,), sig=c.sV, drain=True)
        est = ph.dma("sync", c.Sdr.ap()[cc * 64:(cc + 1) * 64, :],
                     ssb[par][:, :], waits=(esb,), sem=qSS[par], tag=f"S{cc}")
        eacd = ph.op("scalar", lambda e, cc=cc: e.activation(
            acd[:, :], c.cdr[0:1, cc * 32:(cc + 1) * 32], AF.Exp),
            waits=(ecr, ev_last), sig=c.sA)
        edrow = ph.op("vector", lambda e: e.tensor_tensor(
            c.drow[:, :], c.drow[:, :], acd[:, :], ALU.mult),
            waits=(eacd,), sig=c.sV, drain=True)
        ecdm = ph.op("tensor", lambda e: e.matmul(
            pM[0:64, 192:224], c.crow[0:1, 0:64], acd[:, :],
            start=True, stop=True), waits=(eacd, esd), sig=c.sT)
        ecdc = ph.op("vector", lambda e: e.tensor_copy(cdc[:, :],
                                                       pM[0:64, 192:224]),
                     waits=(ecdm,), sig=c.sV, drain=True)
        ec1 = ph.op("vector", lambda e: e.tensor_tensor(
            t1c[:, :].rearrange("p (h q) -> p h q", h=N_HEADS),
            c.carry[:, :].rearrange("p (h q) -> p h q", h=N_HEADS),
            cdc[:, :].unsqueeze(2).broadcast_to([64, N_HEADS, DP]), ALU.mult),
            waits=(ecdc,), sig=c.sV, drain=True)
        ev_last = ph.op("vector", lambda e, par=par: e.tensor_tensor(
            c.carry[:, :], t1c[:, :], ssb[par][:, :], ALU.add),
            waits=(ec1, esb), sig=c.sV, drain=True)
        ev_ssb_used[par] = (ev_last, est)

    ep1 = ph.dma("gpsimd",
                 c.cc_in.ap()[0:2048, :].rearrange("(h n) p -> n h p", n=64),
                 c.carry[:, :].rearrange("n (h p) -> n h p", h=N_HEADS),
                 waits=(ev_last,), tag="ccp1")
    ep2 = ph.dma("gpsimd", c.cc_in.ap()[2048:2049, 0:32], c.drow[:, :],
                 waits=(edrow, ep1), sem=qCP, tag="ccp2")
    c.cc_ev = ph.op("gpsimd", lambda e: e.collective_compute(
        "AllGather", ALU.bypass,
        replica_groups=[list(range(c.n_cc))],
        ins=[c.cc_in.ap().opt()],
        outs=[c.cc_out.ap()[0:2080 * c.n_cc, :].opt()]),
        waits=(ep2,), sig=c.cs, amt=1)
    ph.emit()


def p6(c):
    ph = Ph(c)
    qID = [c.mk_qsem(f"p6qID{i}", "sync") for i in range(2)]
    qGA = [c.mk_qsem(f"p6qGA{i}", "gpsimd") for i in range(2)]
    qUP = [c.mk_qsem(f"p6qUP{i}", "gpsimd") for i in range(2)]
    qWD = [c.mk_qsem(f"p6qWD{i}", "gpsimd") for i in range(4)]
    qSA = c.mk_qsem("p6qSA", "gpsimd")
    qDB = c.mk_qsem("p6qDB", "sync")

    idc = [ph.sb(f"p6idc{i}", [128, NSL, 2], F32) for i in range(2)]
    sidx = [ph.sb(f"p6sx{i}", [128, NSL, 2], I32) for i in range(2)]
    Xg = [ph.sb(f"p6xg{i}", [128, DM], BF16) for i in range(2)]
    idxg = [ph.sb(f"p6ixg{i}", [128, 2], I32) for i in range(2)]
    idxs = [ph.sb(f"p6ixs{i}", [128, 2], I32) for i in range(2)]
    XgF = ph.sb("p6xgf", [128, NJ, CAPE], BF16)
    aF = ph.sb("p6af", [128, NIC, CAPE], BF16)
    wup = [ph.sb(f"p6wu{i}", [128, NJ, 128], BF16) for i in range(2)]
    wdn = ph.sb("p6wd", [128, NIC, DM], BF16)
    gtmp = ph.sb("p6gt", [128, CAPE], F32)
    dout = [ph.sb(f"p6do{i}", [128, DM], F32) for i in range(2)]
    pU = [ph.ps(f"p6pU{i}", [128, 512], F32) for i in range(2)]
    pXT = [ph.ps(f"p6pXT{i}", [128, 1024], BF16) for i in range(2)]
    pD = ph.ps("p6pD", [128, DM], F32)

    einit6 = ph.op("vector", lambda e: (
        c.dr(e.memset(sidx[0][:, :, :], 0)),
        e.memset(sidx[1][:, :, :], 0))[-1], sig=c.sV, drain=True)
    ev_id = [None, None]
    used_id = [None, None]
    ev_sx = [einit6, einit6]
    ev_ga = [None, None]
    used_xg = [None, None]
    ev_tp = [None, None]
    ev_xgf = None
    ev_up = [None, None]
    used_wup = [None, None]
    ev_gt = None
    ev_af = None
    ev_wd = [None] * 4
    ev_do = [None, None]
    ev_do_last = None
    ev_sa = None
    last_aF_use = None
    last_XgF_use = None
    last_wd_use = None
    for ex in range(NE):
        epar = ex % 2
        ev_id[epar] = ph.dma(
            "sync", idc[epar][:, :, :],
            c.idxcw.ap()[ex * CAPE:(ex + 1) * CAPE, :]
            .rearrange("(s p) t -> p s t", p=128),
            waits=(used_id[epar],), sem=qID[epar], tag=f"6id{ex}")
        ev_sx[epar] = ph.op("vector", lambda e, epar=epar: e.tensor_copy(
            sidx[epar][:, :, :], idc[epar][:, :, :]),
            waits=(ev_id[epar], ev_sx[epar], ev_sa if c.sim_safe else None),
            sig=c.sV, drain=True)
        wdev = []
        for ic in range(NIC):
            wdev.append(ph.dma(
                "gpsimd", wdn[:, ic, :], c.EDN.ap()[ex, ic],
                waits=(ev_wd[ic % 4], last_wd_use), sem=qWD[ic % 4],
                tag=f"6wd{ex}_{ic}"))
            ev_wd[ic % 4] = wdev[-1]
        for s in range(NSL):
            spar = s % 2
            eixg = ph.op("vector", lambda e, spar=spar, s=s, epar=epar:
                         e.tensor_copy(idxg[spar][:, :], sidx[epar][:, s, :]),
                         waits=(ev_sx[epar], ev_ga[spar]), sig=c.sV, drain=True)
            ev_ga[spar] = ph.op("gpsimd", lambda e, spar=spar:
                                e.indirect_dma_start(
                out=Xg[spar][:, :], out_offset=None, in_=c.xn_tok.ap(),
                in_offset=IndirectOffsetOnAxis(idxg[spar][:, 0:1], axis=0)),
                waits=(eixg, used_xg[spar],
                       ev_ga[spar] if c.sim_safe else None),
                sig=qGA[spar], amt=16)
            for hb in range(2):
                def tpf(e, spar=spar, hb=hb):
                    for jj in range(8):
                        j = hb * 8 + jj
                        i_ = e.transpose(pXT[hb][:, jj * 128:(jj + 1) * 128],
                                         Xg[spar][:, j * 128:(j + 1) * 128],
                                         c.idb[:, :])
                    return i_
                etp = ph.op("tensor", tpf,
                            waits=(ev_ga[spar], ev_tp[hb], last_XgF_use), sig=c.sT)
                ev_tp[hb] = ph.op("scalar", lambda e, s=s, hb=hb: e.activation(
                    XgF[:, hb * 8:hb * 8 + 8, s * 128:(s + 1) * 128],
                    pXT[hb][:, :].rearrange("p (a t) -> p a t", a=8),
                    AF.Copy), waits=(etp,), sig=c.sA)
            used_xg[spar] = ev_tp[1]
        ev_xgf = (ev_tp[0], ev_tp[1])
        last_XgF_use = None
        OCORD = []
        for gg in range(NIC):
            OCORD.append(gg)
            OCORD.append(gg + NIC)
        pU_free = [None, None]
        for oi, oc in enumerate(OCORD):
            opar = oi % 2
            ev_up[opar] = ph.dma(
                "gpsimd", wup[opar][:, :, :], c.EUP.ap()[ex, oc],
                waits=(used_wup[opar],), sem=qUP[opar], tag=f"6up{ex}_{oc}")
            is_g = oc < NIC
            for jt in range(3):
                J = 512 if jt < 2 else 256
                j0 = jt * 512
                ppar = (oi * 3 + jt) % 2

                def umm(e, opar=opar, ppar=ppar, j0=j0, J=J):
                    for j in range(NJ):
                        i_ = e.matmul(pU[ppar][:, 0:J], wup[opar][:, j, :],
                                      XgF[:, j, j0:j0 + J],
                                      start=(j == 0), stop=(j == NJ - 1))
                    return i_
                emu = ph.op("tensor", umm,
                            waits=(ev_up[opar], ev_xgf, pU_free[ppar]),
                            sig=c.sT)
                used_wup[opar] = emu
                last_XgF_use = emu
                if is_g:
                    ev_gt = ph.op("scalar", lambda e, ppar=ppar, j0=j0, J=J:
                                  e.activation(gtmp[:, j0:j0 + J],
                                               pU[ppar][:, 0:J], AF.Silu),
                                  waits=(emu, ev_gt), sig=c.sA)
                    pU_free[ppar] = ev_gt
                else:
                    ev_af = ph.op("vector", lambda e, ppar=ppar, j0=j0, J=J,
                                  oc=oc: e.tensor_tensor(
                        aF[:, oc - NIC, j0:j0 + J], pU[ppar][:, 0:J],
                        gtmp[:, j0:j0 + J], ALU.mult),
                        waits=(emu, ev_gt, last_aF_use), sig=c.sV, drain=True)
                    pU_free[ppar] = ev_af
                    if ex == 0 and oc == NIC and jt == 2:
                        d5 = ph.dma("sync", c.dbg6b.ap()[:, 3072:4096],
                                    XgF[:, 0, 0:1024], waits=(ev_af,),
                                    sem=qDB, tag="dbgXgF")
                        d6b = ph.dma("sync", c.dbg6.ap() if False else
                                     c.dbg7.ap()[:, 0:1280], gtmp[:, 0:1280],
                                     waits=(d5,), sem=qDB, tag="dbggt")
                        d7 = ph.dma("gpsimd", c.dbg7.ap()[:, 1280:2560],
                                    aF[:, 0, :], waits=(d6b,), tag="dbgaf")
                        last_aF_use = (last_aF_use, d7)
                        last_XgF_use = (last_XgF_use, d5)
        last_aF_use = None
        for s in range(NSL):
            spar = s % 2

            def dmm(e, s=s):
                for ic in range(NIC):
                    for oq in range(4):
                        i_ = e.matmul(pD[:, oq * 512:(oq + 1) * 512],
                                      aF[:, ic, s * 128:(s + 1) * 128],
                                      wdn[:, ic, oq * 512:(oq + 1) * 512],
                                      start=(ic == 0), stop=(ic == NIC - 1))
                return i_
            edm = ph.op("tensor", dmm,
                        waits=(ev_af, *wdev, ev_do_last), sig=c.sT)
            last_aF_use = edm
            last_wd_use = edm
            ev_do_last = ph.op("vector", lambda e, spar=spar, s=s, epar=epar:
                               e.tensor_scalar(dout[spar][:, :], pD[:, :],
                                               idc[epar][:, s, 1:2], None,
                                               ALU.mult),
                               waits=(edm, ev_do[spar]), sig=c.sV, drain=True)
            ev_do[spar] = ev_do_last
            if ex == 0 and s == 0:
                d1 = ph.dma("sync", c.dbg6.ap(), dout[spar][:, :],
                            waits=(ev_do_last,), sem=qDB, tag="dbg6")
                d2 = ph.dma("sync", c.dbg6b.ap()[:, 0:2048], Xg[spar][:, :],
                            waits=(d1,), sem=qDB, tag="dbg6b1")
                d3 = ph.dma("sync", c.dbg6b.ap()[:, 2048:2560],
                            XgF[:, 0, 0:512], waits=(d2,), sem=qDB,
                            tag="dbg6b2")
                d4 = ph.dma("sync", c.dbg6b.ap()[:, 2560:3072],
                            aF[:, 0, 0:512], waits=(d3,), sem=qDB,
                            tag="dbg6b3")
                ev_do_last = (ev_do_last, d4)
                used_xg[spar] = (used_xg[spar], d4)
            eixs = ph.op("vector", lambda e, spar=spar, s=s, epar=epar:
                         e.tensor_copy(idxs[spar][:, :], sidx[epar][:, s, :]),
                         waits=(ev_sx[epar], ev_sa), sig=c.sV, drain=True)
            ev_sa = ph.op("gpsimd", lambda e, spar=spar:
                          e.indirect_dma_start(
                out=c.gacc.ap(),
                out_offset=IndirectOffsetOnAxis(idxs[spar][:, 0:1], axis=0),
                in_=dout[spar][:, :], in_offset=None, compute_op=ALU.add),
                waits=(eixs, ev_do_last,
                       ev_sa if (c.sim_safe or s == 0) else None),
                sig=qSA, amt=16)
            used_id[epar] = ev_sa
            ev_do[spar] = (ev_do[spar], ev_sa)
    ph.emit()


def cccomb(c):
    ph = Ph(c)
    qF = [c.mk_qsem(f"c6qF{i}", "sync") for i in range(2)]
    qD = c.mk_qsem("c6qD", "sync")
    Fb = [ph.sb(f"c6F{i}", [64, DM], F32) for i in range(2)]
    db = ph.sb("c6d", [1, 32], F32)
    cdc = ph.sb("c6cdc", [64, 32], F32)
    cand = ph.sb("c6cand", [64, DM], F32)
    acc = ph.sb("c6acc", [64, DM], F32)
    tt = ph.sb("c6tt", [64, DM], F32)
    pC = ph.ps("c6pC", [64, 32], F32)

    ccev = c.cc_ev
    e0 = ph.op("vector", lambda e: (c.dr(e.memset(acc[:, :], 0.0)),
                                    e.memset(cand[:, :], 0.0))[-1],
               sig=c.sV, drain=True)
    ev_f = [None, None]
    used_f = [None, None]
    ev_d = None
    ev_cd = None
    prev = e0
    for s in range(2):
        if s == 1:
            prev = ph.op("vector", lambda e: e.memset(cand[:, :], 0.0),
                         waits=(prev,), sig=c.sV, drain=True)
        for k in range(1, 4):
            r = 4 * s + k
            par = k % 2
            ev_f[par] = ph.dma(
                "sync", Fb[par][:, :].rearrange("n (h p) -> n h p", h=N_HEADS),
                c.cc_out.ap()[(r - 1) * 2080:(r - 1) * 2080 + 2048, :]
                .rearrange("(h n) p -> n h p", n=64),
                waits=(ccev, used_f[par]), sem=qF[par], tag=f"cF{r}")
            ev_d = ph.dma("sync", db[:, :],
                          c.cc_out.ap()[(r - 1) * 2080 + 2048:
                                        (r - 1) * 2080 + 2049, 0:32],
                          waits=(ccev, ev_cd), sem=qD, tag=f"cD{r}")
            ecdm = ph.op("tensor", lambda e: e.matmul(
                pC[:, :], c.crow[0:1, 0:64], db[:, :], start=True, stop=True),
                waits=(ev_d, ev_cd), sig=c.sT)
            ev_cd = ph.op("vector", lambda e: e.tensor_copy(cdc[:, :], pC[:, :]),
                          waits=(ecdm,), sig=c.sV, drain=True)
            def step(e, par=par, r=r):
                d = c.dr
                d(e.tensor_tensor(tt[:, :].rearrange("n (h p) -> n h p", h=N_HEADS),
                                  cand[:, :].rearrange("n (h p) -> n h p", h=N_HEADS),
                                  cdc[:, :].unsqueeze(2).broadcast_to([64, N_HEADS, DP]),
                                  ALU.mult))
                d(e.tensor_tensor(cand[:, :], tt[:, :], Fb[par][:, :], ALU.add))
                d(e.tensor_scalar(tt[:, :], cand[:, :], c.msk[:, r:r + 1], None,
                                  ALU.mult))
                return e.tensor_tensor(acc[:, :], acc[:, :], tt[:, :], ALU.add)
            prev = ph.op("vector", step, waits=(ev_cd, ev_f[par], prev),
                         sig=c.sV, drain=True)
            used_f[par] = prev
    efin = ph.op("vector", lambda e: (c.dr(e.tensor_copy(c.carry[:, :], acc[:, :])),
                                      e.tensor_copy(c.carrybf[:, :], acc[:, :]))[-1],
                 waits=(prev,), sig=c.sV, drain=True)
    ph.emit()


def p3b4(c):
    ph = Ph(c)
    qB = [c.mk_qsem(f"p4qB{i}", "sync") for i in range(2)]
    qC = [c.mk_qsem(f"p4qC{i}", "sync") for i in range(2)]
    qX = [c.mk_qsem(f"p4qX{i}", "sync") for i in range(2)]
    qS = [c.mk_qsem(f"p4qS{i}", "sync") for i in range(2)]
    qXF = [c.mk_qsem(f"p4qXF{i}", "sync") for i in range(2)]
    qZ = [c.mk_qsem(f"p4qZ{i}", "sync") for i in range(2)]
    qSO = [c.mk_qsem(f"p4qSO{i}", "sync") for i in range(2)]
    qWO = [c.mk_qsem(f"p4qWO{i}", "gpsimd") for i in range(4)]

    bfc = [ph.sb(f"p4b{i}", [64, N_HEADS, 128], BF16) for i in range(2)]
    cfc = [ph.sb(f"p4c{i}", [64, N_HEADS, 128], BF16) for i in range(2)]
    xtk = [ph.sb(f"p4x{i}", [128, DM], BF16) for i in range(2)]
    ssb = [ph.sb("p4s0", [64, DM], F32)] * 2
    xfc = [ph.sb("p4xf0", [128, NJ, 128], F32)] * 2
    zsc = [ph.sb("p4zs0", [128, NJ, 128], F32)] * 2
    wot = ph.sb("p4wot", [128, NJ, DM], BF16)
    etk = ph.sb("p4etk", [128, 32], F32)
    dgeb = [ph.sb(f"p4dge{i}", [128, 512], F32) for i in range(2)]
    esb = [ph.sb(f"p4esb{i}", [128, 512], F32) for i in range(2)]
    w1b = [ph.sb(f"p4w1{i}", [128, 512], BF16) for i in range(2)]
    ytok = ph.sb("p4ytok", [128, DM], F32)
    yg = ph.sb("p4yg", [128, NJ, 128], BF16)
    ygf = ph.sb("p4ygf", [128, 128], F32)
    stile = [ph.sb("p4st0", [128, DM], F32)] * 2
    acd = ph.sb("p4acd", [1, 32], F32)
    cdc = ph.sb("p4cdc", [64, 32], F32)
    t1c = ph.sb("p4t1c", [64, DM], F32)
    pR = ph.ps("p4pR", [128, 512], F32)
    pCB = ph.ps("p4pCB", [128, 512], F32)
    pIA = ph.ps("p4pIA", [128, 1024], F32)
    pIB = ph.ps("p4pIB", [128, 1024], F32)

    ev_wo = []
    for j in range(NJ):
        ev_wo.append(ph.dma("gpsimd", wot[:, j, :], c.WOT.ap()[j],
                            sem=qWO[j % 4], tag=f"wo{j}",
                            waits=(ev_wo[j - 4],) if j >= 4 else ()))
    ev_ld = {k: [None, None] for k in "bcxszf"}
    used = {k: [None, None] for k in "bcxszf"}
    ev_dge = [None, None]
    ev_esb = [None, None]
    ev_w1 = [None, None]
    ev_etk = None
    ev_carry = None
    ev_so = [None, None]
    ev_dupf = [None, None]
    prev_yev_final = None
    for cc in range(NCH):
        par = cc % 2
        sl = slice(cc * 128, (cc + 1) * 128)
        ev_ld["b"][par] = ph.dma(
            "sync", bfc[par][:, :, :],
            c.Bf.ap()[:, sl].rearrange("(h n) t -> n h t", n=64),
            waits=(used["b"][par],), sem=qB[par], tag=f"4b{cc}")
        ev_ld["c"][par] = ph.dma(
            "sync", cfc[par][:, :, :],
            c.Cf.ap()[:, sl].rearrange("(h n) t -> n h t", n=64),
            waits=(used["c"][par],), sem=qC[par], tag=f"4c{cc}")
        ev_ld["x"][par] = ph.dma(
            "sync", xtk[par][:, :], c.xFbf.ap()[:, sl], transpose=True,
            waits=(used["x"][par],), sem=qX[par], tag=f"4x{cc}")
        ev_ld["s"][par] = ph.dma(
            "sync", ssb[par][:, :], c.Sdr.ap()[cc * 64:(cc + 1) * 64, :],
            waits=(used["s"][par], used["s"][1 - par]), sem=qS[par],
            tag=f"4s{cc}")
        ev_ld["f"][par] = ph.dma(
            "sync", xfc[par][:, :, :],
            c.xF32.ap()[:, sl].rearrange("(j p) t -> p j t", p=128),
            waits=(used["f"][par], used["f"][1 - par]), sem=qXF[par],
            tag=f"4f{cc}")
        ev_ld["z"][par] = ph.dma(
            "sync", zsc[par][:, :, :],
            c.zs.ap()[:, sl].rearrange("(j p) t -> p j t", p=128),
            waits=(used["z"][par], used["z"][1 - par]), sem=qZ[par],
            tag=f"4z{cc}")

        ev_etk = ph.op("scalar", lambda e, cc=cc: e.activation(
            etk[:, :], c.cumTok[:, cc * 32:(cc + 1) * 32], AF.Exp),
            waits=(ev_etk, prev_yev_final), sig=c.sA)

        ev_yev = None
        last_ymm = None
        for half in (range(2) if getattr(c, "p4_y", True) else ()):
            hymm = []
            for gqi in range(4):
                gq = half * 4 + gqi
                gpar = gq % 2
                h0 = gq * 4
                csl = slice(cc * 32 + h0, cc * 32 + h0 + 4)
                ev_dge[gpar] = ph.op(
                    "vector", lambda e, csl=csl, gpar=gpar: e.tensor_tensor(
                        dgeb[gpar][:, :].rearrange("p (u q) -> p u q", u=4),
                        c.i4[:, :].rearrange("p (u q) -> p u q", u=4),
                        c.cumTok[:, csl].unsqueeze(2)
                        .broadcast_to([128, 4, 128]), ALU.mult),
                    waits=(ev_dge[gpar],), sig=c.sV, drain=True)

                def rmm(e, csl=csl, gpar=gpar):
                    e.matmul(pR[:, :], c.lt[:, :], dgeb[gpar][:, :],
                             start=True, stop=False)
                    e.matmul(pR[:, :], c.ut[:, :], c.nd4[:, :],
                             start=False, stop=False)
                    return e.matmul(
                        pR[:, :].rearrange("p (u q) -> p u q", u=4),
                        c.negi[:, :],
                        c.cumTok[:, csl].unsqueeze(2)
                        .broadcast_to([128, 4, 128]),
                        start=False, stop=True)
                erm = ph.op("tensor", rmm,
                            waits=(ev_dge[gpar], ev_esb[0], ev_esb[1]),
                            sig=c.sT)
                ev_esb[gpar] = ph.op("scalar", lambda e, gpar=gpar: e.activation(
                    esb[gpar][:, :], pR[:, :], AF.Exp), waits=(erm,), sig=c.sA)

                if getattr(c, "p4_part", "all") == "r":
                    hymm.append(erm)
                    continue

                def cbm(e, par=par, h0=h0):
                    for u in range(4):
                        h = h0 + u
                        i_ = e.matmul(pCB[:, u * 128:(u + 1) * 128],
                                      bfc[par][:, h, :], cfc[par][:, h, :],
                                      start=True, stop=True)
                    return i_
                ecb = ph.op("tensor", cbm,
                            waits=(ev_ld["b"][par], ev_ld["c"][par],
                                   ev_w1[0], ev_w1[1]), sig=c.sT)
                ev_w1[gpar] = ph.op("vector", lambda e, gpar=gpar: e.tensor_tensor(
                    w1b[gpar][:, :], pCB[:, :], esb[gpar][:, :], ALU.mult),
                    waits=(ecb, ev_esb[gpar]), sig=c.sV, drain=True)

                if getattr(c, "p4_part", "all") == "cb":
                    hymm.append(ecb)
                    continue

                def ymm(e, par=par, h0=h0, gpar=gpar):
                    for u in range(4):
                        h = h0 + u
                        off = (h % 16) * 64
                        e.matmul(pIA[:, off:off + 64],
                                 w1b[gpar][:, u * 128:(u + 1) * 128],
                                 xtk[par][:, h * DP:(h + 1) * DP],
                                 start=True, stop=True)
                        i_ = e.matmul(pIB[:, off:off + 64],
                                      cfc[par][:, h, :],
                                      c.carrybf[:, h * DP:(h + 1) * DP],
                                      start=True, stop=True)
                    return i_
                emm = ph.op("tensor", ymm,
                            waits=(ev_w1[gpar], ev_ld["x"][par], ev_carry,
                                   ev_yev), sig=c.sT)
                hymm.append(emm)
            if getattr(c, "p4_part", "all") in ("r", "cb"):
                last_ymm = hymm[-1]
                ev_yev = hymm[-1]
                prev_yev_final = hymm[-1]
                continue
            def yev(e, half=half, cc=cc):
                d = c.dr
                ytmp = stile[0][:, 0:1024]
                d(e.tensor_tensor(
                    ytmp.rearrange("p (h q) -> p h q", h=16),
                    pIB[:, :].rearrange("p (h q) -> p h q", h=16),
                    etk[:, half * 16:half * 16 + 16].unsqueeze(2)
                    .broadcast_to([128, 16, DP]), ALU.mult))
                return e.tensor_tensor(ytok[:, half * 1024:half * 1024 + 1024],
                                       ytmp, pIA[:, :], ALU.add)
            ev_yev = ph.op("vector", yev,
                           waits=(hymm[-1], ev_etk, ev_so[0], ev_so[1]),
                           sig=c.sV, drain=True)
            last_ymm = hymm[-1]
            prev_yev_final = ev_yev
        used["b"][par] = last_ymm
        used["c"][par] = last_ymm
        used["x"][par] = last_ymm

        eacd = ph.op("scalar", lambda e, cc=cc: e.activation(
            acd[:, :], c.cdr[0:1, cc * 32:(cc + 1) * 32], AF.Exp),
            waits=(ev_etk,), sig=c.sA)
        ecdm = ph.op("tensor", lambda e: e.matmul(
            pR[0:64, 448:480], c.crow[0:1, 0:64], acd[:, :],
            start=True, stop=True),
            waits=(eacd, ev_esb[0], ev_esb[1]), sig=c.sT)
        ecdc = ph.op("vector", lambda e: e.tensor_copy(cdc[:, :],
                                                       pR[0:64, 448:480]),
                     waits=(ecdm,), sig=c.sV, drain=True)
        ec1 = ph.op("vector", lambda e: e.tensor_tensor(
            t1c[:, :].rearrange("p (h q) -> p h q", h=N_HEADS),
            c.carry[:, :].rearrange("p (h q) -> p h q", h=N_HEADS),
            cdc[:, :].unsqueeze(2).broadcast_to([64, N_HEADS, DP]), ALU.mult),
            waits=(ecdc,), sig=c.sV, drain=True)
        ec2 = ph.op("vector", lambda e, par=par: e.tensor_tensor(
            c.carry[:, :], t1c[:, :], ssb[par][:, :], ALU.add),
            waits=(ec1, ev_ld["s"][par]), sig=c.sV, drain=True)
        used["s"][par] = ec2
        ecb0 = ph.op("vector", lambda e: e.tensor_copy(
            c.carrybf[:, :], c.carry[:, :]),
            waits=(ec2, last_ymm), sig=c.sV, drain=True)
        ev_carry = ecb0

        if not getattr(c, "p4_tail", True):
            used["f"][par] = ev_ld["f"][par]
            used["z"][par] = ev_ld["z"][par]
            ev_yev = ev_yev
            continue
        ev_tp = [None, None]
        ev_ygj = None
        for j in range(NJ):
            tb = pR if j % 2 == 0 else pCB
            etp = ph.op("tensor", lambda e, j=j, tb=tb: e.transpose(
                tb[:, 0:128], ytok[:, j * 128:(j + 1) * 128], c.idf[:, :]),
                waits=(ev_yev if j == 0 else None, ev_tp[j % 2],
                       ecdc if j < 2 else None), sig=c.sT)

            def ygj(e, j=j, tb=tb, par=par):
                c.dr(e.scalar_tensor_tensor(
                    ygf[:, :], xfc[par][:, j, :], c.df[:, j:j + 1],
                    tb[:, 0:128], ALU.mult, ALU.add))
                return e.tensor_tensor(yg[:, j, :], ygf[:, :],
                                       zsc[par][:, j, :], ALU.mult)
            ev_tp[j % 2] = ph.op("vector", ygj,
                                 waits=(etp, ev_ld["f"][par], ev_ld["z"][par],
                                        ev_ygj), sig=c.sV, drain=True)
            ev_ygj = ev_tp[j % 2]
        used["f"][par] = ev_ygj
        used["z"][par] = ev_ygj

        def omm(e, par=par):
            for j in range(NJ):
                for oq in range(4):
                    tgt = pIA if oq < 2 else pIB
                    off = (oq % 2) * 512
                    i_ = e.matmul(tgt[:, off:off + 512], yg[:, j, :],
                                  wot[:, j, oq * 512:(oq + 1) * 512],
                                  start=(j == 0), stop=(j == NJ - 1))
            return i_
        eo = ph.op("tensor", omm, waits=(ev_ygj, ev_yev,
                                         Ev(qWO[0], ph.items[0]) if False else None,
                                         *[ev_wo[jj] for jj in (12, 13, 14, 15)],
                                         ev_so[par]), sig=c.sT)

        def sev(e, par=par):
            c.dr(e.tensor_copy(stile[par][:, 0:1024], pIA[:, :]))
            return e.tensor_copy(stile[par][:, 1024:2048], pIB[:, :])
        esv = ph.op("vector", sev, waits=(eo, ev_so[1 - par]), sig=c.sV,
                    drain=True)
        ev_so[par] = ph.dma("sync", c.s_tok.ap()[sl, :], stile[par][:, :],
                            waits=(esv,), sem=qSO[par], tag=f"4so{cc}")
        ev_yev = esv
        ev_carry = (ev_carry, esv)
    ph.emit()


def p7a(c):
    ph = Ph(c)
    qXN = [c.mk_qsem(f"p7qXN{i}", "sync") for i in range(4)]
    qUP = [c.mk_qsem(f"p7qUP{i}", "gpsimd") for i in range(2)]
    qAF = [c.mk_qsem(f"p7qAF{i}", "sync") for i in range(2)]
    xnf = ph.sb("p7xnf", [128, NJ, T], BF16)
    wup = [ph.sb(f"p7wu{i}", [128, NJ, 128], BF16) for i in range(2)]
    gtmp = ph.sb("p7gt", [128, T], F32)
    afb = [ph.sb(f"p7af{i}", [128, T], BF16) for i in range(2)]
    pU = [ph.ps(f"p7pU{i}", [128, 512], F32) for i in range(2)]

    ev_xn = []
    for q in range(4):
        ev_xn.append(ph.dma(
            "sync", xnf[:, 4 * q:4 * q + 4, :],
            c.xnF.ap()[:, 0:T].rearrange("(j p) t -> p j t", p=128)
            [:, 4 * q:4 * q + 4, :], sem=qXN[q], tag=f"7xn{q}"))
    ev_up = [None, None]
    used_wup = [None, None]
    ev_gt = None
    ev_afs = [None, None]
    ev_afw = [None, None]
    pU_free = [None, None]
    pucnt = 0
    OCORD = []
    for gg in range(NSDC):
        OCORD.append(gg)
        OCORD.append(gg + NSDC)
    for oi, oc in enumerate(OCORD):
        opar = oi % 2
        is_g = oc < NSDC
        ev_up[opar] = ph.dma(
            "gpsimd", wup[opar][:, :, :], c.SUP.ap()[oc],
            waits=(used_wup[opar],), sem=qUP[opar], tag=f"7up{oc}")
        for tt in range(NTT):
            ppar = pucnt % 2
            pucnt += 1

            def umm(e, opar=opar, ppar=ppar, tt=tt):
                for j in range(NJ):
                    i_ = e.matmul(pU[ppar][:, :], wup[opar][:, j, :],
                                  xnf[:, j, tt * 512:(tt + 1) * 512],
                                  start=(j == 0), stop=(j == NJ - 1))
                return i_
            emu = ph.op("tensor", umm,
                        waits=(ev_up[opar], tuple(ev_xn) if oi == 0 else None,
                               pU_free[ppar]), sig=c.sT)
            used_wup[opar] = emu
            if is_g:
                ev_gt = ph.op("scalar", lambda e, ppar=ppar, tt=tt:
                              e.activation(gtmp[:, tt * 512:(tt + 1) * 512],
                                           pU[ppar][:, :], AF.Silu),
                              waits=(emu, ev_gt), sig=c.sA)
                pU_free[ppar] = ev_gt
            else:
                ev_afs[opar] = ph.op("vector", lambda e, ppar=ppar, tt=tt,
                                     opar=opar: e.tensor_tensor(
                    afb[opar][:, tt * 512:(tt + 1) * 512], pU[ppar][:, :],
                    gtmp[:, tt * 512:(tt + 1) * 512], ALU.mult),
                    waits=(emu, ev_gt, ev_afw[opar] if tt == 0 else None),
                    sig=c.sV, drain=True)
                pU_free[ppar] = ev_afs[opar]
        if not is_g:
            row = (oc - NSDC) * 128
            ev_afw[opar] = ph.dma(
                "sync", c.aFs.ap()[row:row + 128, :], afb[opar][:, :],
                waits=(ev_afs[opar],), sem=qAF[opar], tag=f"7af{oc}")
    ph.emit()


def p8(c):
    ph = Ph(c)
    qAF = [c.mk_qsem(f"p8qAF{i}", "sync") for i in range(2)]
    qGC = [c.mk_qsem(f"p8qGC{i}", "sync") for i in range(2)]
    qST = [c.mk_qsem(f"p8qST{i}", "sync") for i in range(2)]
    qXT = [c.mk_qsem(f"p8qXT{i}", "sync") for i in range(2)]
    qWD = [c.mk_qsem(f"p8qWD{i}", "gpsimd") for i in range(4)]
    qOU = [c.mk_qsem(f"p8qOU{i}", "sync") for i in range(2)]
    sdn = ph.sb("p8sdn", [128, NSDC, DM], BF16)
    aft = [ph.sb(f"p8af{i}", [128, NSDC, 128], BF16) for i in range(2)]
    gct = [ph.sb(f"p8gc{i}", [128, DM], F32) for i in range(2)]
    stt = [ph.sb(f"p8st{i}", [128, DM], F32) for i in range(2)]
    xtt = [ph.sb(f"p8xt{i}", [128, DM], F32) for i in range(2)]
    fft = ph.sb("p8ff", [128, DM], F32)
    gsl = ph.sb("p8gs", [128, DM], F32)
    ott = [ph.sb(f"p8ot{i}", [128, DM], F32) for i in range(2)]
    pG = [ph.ps(f"p8pG{i}", [128, DM], F32) for i in range(2)]

    ev_wd = [None] * 4
    for ic in range(NSDC):
        ev_wd[ic % 4] = ph.dma("gpsimd", sdn[:, ic, :], c.SDN.ap()[ic],
                               waits=(ev_wd[ic % 4],), sem=qWD[ic % 4],
                               tag=f"8wd{ic}")
    wall = tuple(ev_wd)
    ld = {k: [None, None] for k in "agsx"}
    used = {k: [None, None] for k in "agsx"}
    ev_mm = [None, None]
    ev_out = [None, None]
    for tt in range(T // 128):
        par = tt % 2
        sl = slice(tt * 128, (tt + 1) * 128)
        ld["a"][par] = ph.dma(
            "sync", aft[par][:, :, :],
            c.aFs.ap()[:, sl].rearrange("(i p) t -> p i t", p=128),
            waits=(used["a"][par],), sem=qAF[par], tag=f"8a{tt}")
        ld["g"][par] = ph.dma("sync", gct[par][:, :], c.gacc.ap()[sl, :],
                              waits=(used["g"][par],), sem=qGC[par],
                              tag=f"8g{tt}")
        ld["s"][par] = ph.dma("sync", stt[par][:, :], c.s_tok.ap()[sl, :],
                              waits=(used["s"][par],), sem=qST[par],
                              tag=f"8s{tt}")
        ld["x"][par] = ph.dma("sync", xtt[par][:, :], c.x_tok.ap()[sl, :],
                              waits=(used["x"][par],), sem=qXT[par],
                              tag=f"8x{tt}")

        def dmm(e, par=par):
            for ic in range(NSDC):
                for oq in range(4):
                    i_ = e.matmul(pG[par][:, oq * 512:(oq + 1) * 512],
                                  aft[par][:, ic, :],
                                  sdn[:, ic, oq * 512:(oq + 1) * 512],
                                  start=(ic == 0), stop=(ic == NSDC - 1))
            return i_
        emm = ph.op("tensor", dmm,
                    waits=(ld["a"][par], wall, ev_mm[par]), sig=c.sT)
        used["a"][par] = emm

        def comb(e, par=par):
            d = c.dr
            d(e.tensor_tensor(fft[:, :], pG[par][:, :], gct[par][:, :],
                              ALU.add))
            return e.engine_nop()
        ecb = ph.op("vector", comb, waits=(emm, ld["g"][par]), sig=c.sV,
                    drain=True)
        used["g"][par] = ecb
        ev_mm[par] = ecb
        egs = ph.op("scalar", lambda e: e.activation(gsl[:, :], fft[:, :],
                                                     AF.Silu),
                    waits=(ecb,), sig=c.sA)

        def fin(e, par=par):
            d = c.dr
            d(e.tensor_tensor(ott[par][:, :], stt[par][:, :], gsl[:, :],
                              ALU.mult))
            return e.tensor_tensor(ott[par][:, :], ott[par][:, :],
                                   xtt[par][:, :], ALU.add)
        efn = ph.op("vector", fin,
                    waits=(egs, ld["s"][par], ld["x"][par], ev_out[par]),
                    sig=c.sV, drain=True)
        used["s"][par] = efn
        used["x"][par] = efn
        ev_out[par] = ph.dma("sync", c.OUT.ap()[sl, :], ott[par][:, :],
                             waits=(efn,), sem=qOU[par], tag=f"8o{tt}")
    ph.emit()



def _consts():
    eye = np.eye(128, dtype=np.float32)
    IDB = eye.astype(BF16NP)
    IDF = eye.copy()
    LT = np.tril(np.ones((128, 128), np.float32))
    UT = np.triu(np.ones((128, 128), np.float32), 1)
    I4 = np.concatenate([eye] * 4, axis=1)
    ND4 = (NEGBIG * I4).astype(np.float32)
    NEGI = (-eye).astype(np.float32)
    CCOL = np.zeros((128, 8), np.float32)
    CCOL[:, 0] = 1.0
    CCOL[127, 1] = 1.0
    CCOL[:, 2] = np.arange(128)
    CCOL[:, 3] = NE * CAPE + np.arange(128)
    CROW = np.zeros((1, 256), np.float32)
    CROW[0, 0:128] = 1.0
    CROW[0, 128:128 + NE] = np.arange(NE) * CAPE
    CAPT = np.broadcast_to((np.arange(NE) * CAPE + CAPE - 1)[None, :],
                           (128, NE)).astype(np.float32).copy()
    NEG1 = np.full((128, NE), -1.0, np.float32)
    TRASHM = (NE * CAPE + np.arange(NE)[None, :] * 128 +
              np.arange(128)[:, None]).astype(np.float32)
    return dict(IDB=IDB, IDF=IDF, LT=LT, UT=UT, I4=I4, ND4=ND4, NEGI=NEGI,
                CCOL=CCOL, CROW=CROW, CAPT=CAPT, NEG1=NEG1, TRASHM=TRASHM)


def _prep_shared(inputs):
    f = lambda a: np.ascontiguousarray(np.asarray(a, dtype=np.float32))
    W_in = f(inputs["L0_attn_W_in"])
    conv_w = f(inputs["L0_attn_conv_w"])
    conv_b = f(inputs["L0_attn_conv_b"])
    D = f(inputs["L0_attn_D"])
    z_bias = f(inputs["L0_attn_z_bias"])
    W_out = f(inputs["L0_attn_W_out"])
    gate_w = f(inputs["L0_ffn_gate_w"])
    eup = f(inputs["L0_ffn_expert_up_w"])
    edn = f(inputs["L0_ffn_expert_down_w"])
    sup = f(inputs["L0_ffn_shared_up_w"])
    sdn = f(inputs["L0_ffn_shared_down_w"])
    d = {}
    d["WINP"] = np.ascontiguousarray(
        W_in[:8192].reshape(64, 128, NJ, 128).transpose(0, 3, 2, 1))
    d["WINPA"] = np.ascontiguousarray(
        W_in[8192:].reshape(32, NJ, 128).transpose(2, 1, 0))
    d["CONVW"] = np.ascontiguousarray(
        conv_w.reshape(OXBC, 128, 4).transpose(1, 0, 2).reshape(128, OXBC * 4))
    d["CONVB"] = np.ascontiguousarray(conv_b.reshape(OXBC, 128).T)
    d["ZBF"] = np.ascontiguousarray(z_bias.reshape(OZ, 128).T)
    d["DF"] = np.ascontiguousarray(np.repeat(D, DP).reshape(NJ, 128).T)
    d["GWT"] = np.ascontiguousarray(gate_w.reshape(NE, NJ, 128).transpose(2, 1, 0))
    d["WOT"] = np.ascontiguousarray(W_out.T.reshape(NJ, 128, DM))
    d["EUP"] = np.ascontiguousarray(
        eup.reshape(NE, NUC, 128, NJ, 128).transpose(0, 1, 4, 3, 2))
    d["EDN"] = np.ascontiguousarray(
        edn.transpose(0, 2, 1).reshape(NE, NIC, 128, DM))
    d["SUP"] = np.ascontiguousarray(
        sup.reshape(NSUC, 128, NJ, 128).transpose(0, 3, 2, 1))
    d["SDN"] = np.ascontiguousarray(sdn.T.reshape(NSDC, 128, DM))
    d.update(_consts())
    return d


def make_in_maps(inputs):
    x = np.asarray(inputs["L0_attn_u"], dtype=np.float32)
    shared = _prep_shared(inputs)
    in_maps = []
    for k in range(8):
        b, seg = divmod(k, 4)
        t0 = seg * T
        halo = (x[b, t0 - HALO:t0, :] if seg > 0
                else np.zeros((HALO, DM), np.float32))
        xTc = np.zeros((DM, XNW), np.float32)
        xTc[:, 0:T] = x[b, t0:t0 + T, :].T
        xTc[:, T:T + HALO] = halo.T
        MSK = np.zeros((64, 8), np.float32)
        MSK[:, k] = 1.0
        m = dict(shared)
        m["xT"] = xTc
        m["x_tok"] = np.ascontiguousarray(x[b, t0:t0 + T, :])
        m["MSK"] = MSK
        in_maps.append(m)
    return in_maps


def kernel(**inputs):
    in_maps = make_in_maps(inputs)
    nc = build_program(0)
    res = run_bass_kernel_spmd(nc, in_maps, core_ids=list(range(8)))
    out = np.empty((2, 4 * T, DM), dtype=np.float32)
    for k in range(8):
        b, seg = divmod(k, 4)
        out[b, seg * T:(seg + 1) * T, :] = res.results[k]["OUT"]
    return out
